# version 01 (final)
# baseline (speedup 1.0000x reference)
import numpy as np
import concourse.bass as bass
import concourse.mybir as mybir
from concourse.bass_utils import run_bass_kernel_spmd
from contextlib import ExitStack

F32 = mybir.dt.float32
BF16 = mybir.dt.bfloat16
AF = mybir.ActivationFunctionType
ALU = mybir.AluOpType
AX = mybir.AxisListType

NCORES = 8
RPC = 4
T = 2048
D = 1024
NC = 8
NT4 = 4
NTT = 16
NEG = -1.0e30


_UID = [0]


def sbuf(nc, nm, shape, dt=F32):
    _UID[0] += 1
    return nc.sbuf_tensor("%s_u%d" % (nm, _UID[0]), shape, dt)


class PB:
    def __init__(self):
        self.items = []

    def em(self, eng, fn, inc=1):
        self.items.append((eng, [fn], inc))

    def grp(self, eng, fns):
        self.items.append((eng, fns, 1))

    def dma(self, out, in_):
        self.em('sync', lambda e, o=out, i=in_: e.dma_start(out=o, in_=i), 16)

    def mm_group(self, ps, pairs):
        fns = []
        n = len(pairs)
        for i, (l, r) in enumerate(pairs):
            fns.append(lambda e, l=l, r=r, s=(i == 0), t=(i == n - 1), ps=ps:
                       e.matmul(ps, l, r, start=s, stop=t))
        self.grp('tensor', fns)

    def mm(self, ps, l, r):
        self.em('tensor', lambda e, ps=ps, l=l, r=r: e.matmul(ps, l, r, start=True, stop=True))

    def cast(self, out, in_):
        self.em('vector', lambda e, o=out, i=in_: e.tensor_copy(out=o, in_=i))

    def finalize(self, nc, es):
        LIMIT = 30000
        epochs, tgts = [], []
        ep, val = 0, 0
        for (eng, fns, inc) in self.items:
            if val + inc > LIMIT and val > 0:
                ep += 1
                val = 0
            epochs.append(ep)
            tgts.append(val)
            val += inc
        finals = {}
        for i in range(len(self.items)):
            finals[epochs[i]] = tgts[i] + self.items[i][2]
        nep = ep + 1
        sems = [es.enter_context(nc.semaphore("tok%d" % k)) for k in range(nep)]
        block = es.enter_context(nc.Block())
        items, N = self.items, len(self.items)

        def stream(engobj, name):
            for i in range(N):
                eng, fns, inc = items[i]
                if eng != name:
                    continue
                e_, t_ = epochs[i], tgts[i]
                if t_ == 0:
                    if e_ > 0:
                        engobj.wait_ge(sems[e_ - 1], finals[e_ - 1])
                else:
                    engobj.wait_ge(sems[e_], t_)
                inst = None
                for fn in fns:
                    inst = fn(engobj)
                inst.then_inc(sems[e_], inc)
            engobj.wait_ge(sems[nep - 1], finals[nep - 1])

        @block.sync
        def _(sync):
            stream(sync, 'sync')

        @block.vector
        def _(vector):
            stream(vector, 'vector')

        @block.scalar
        def _(scalar):
            stream(scalar, 'scalar')

        @block.tensor
        def _(tensor):
            stream(tensor, 'tensor')


def winap(tile_ap, col0, nti, ndd=32):
    return bass.AP(tile_ap.tensor, tile_ap.offset + col0,
                   [[tile_ap.ap[0][0], tile_ap.ap[0][1]], [1, nti], [1, ndd]])


def v3(t_ap, b=32):
    return t_ap.rearrange("p (a b) -> p a b", b=b)


def emit_stats(p, src_fn, sq, psS, rows, eps=1e-5):
    rsum, rsq, ra, rb = rows
    for c in range(NC):
        ap = src_fn(c)
        for tw in range(NT4):
            p.em('tensor', lambda e, ps=psS[tw][0:1, :], r=ap[:, tw * 512:(tw + 1) * 512],
                 s=(c == 0), t=(c == NC - 1), on=p.ones_col: e.matmul(ps, on[:, 0:1], r, start=s, stop=t))
    for tw in range(NT4):
        p.em('scalar', lambda e, o=rsum[:, tw * 512:(tw + 1) * 512], i=psS[tw][0:1, :]: e.copy(out=o, in_=i))
    for c in range(NC):
        ap = src_fn(c)
        p.em('scalar', lambda e, o=sq[:, :], i=ap: e.activation(out=o, in_=i, func=AF.Square))
        for tw in range(NT4):
            p.em('tensor', lambda e, ps=psS[tw][0:1, :], r=sq[:, tw * 512:(tw + 1) * 512],
                 s=(c == 0), t=(c == NC - 1), on=p.ones_col: e.matmul(ps, on[:, 0:1], r, start=s, stop=t))
    for tw in range(NT4):
        p.em('scalar', lambda e, o=rsq[:, tw * 512:(tw + 1) * 512], i=psS[tw][0:1, :]: e.copy(out=o, in_=i))
    inv = 1.0 / D
    p.em('vector', lambda e: e.tensor_scalar(out=rsum[:, :], in0=rsum[:, :], scalar1=inv, scalar2=None, op0=ALU.mult))
    p.em('vector', lambda e: e.tensor_scalar(out=rsq[:, :], in0=rsq[:, :], scalar1=inv, scalar2=None, op0=ALU.mult))
    p.em('vector', lambda e: e.tensor_tensor(out=ra[:, :], in0=rsum[:, :], in1=rsum[:, :], op=ALU.mult))
    p.em('vector', lambda e: e.tensor_tensor(out=rsq[:, :], in0=rsq[:, :], in1=ra[:, :], op=ALU.subtract))
    p.em('scalar', lambda e: e.activation(out=ra[:, :], in_=rsq[:, :], func=AF.Sqrt, bias=p.eps_ap))
    p.em('vector', lambda e: e.reciprocal(out=ra[:, :], in_=ra[:, :]))
    p.em('vector', lambda e: e.tensor_tensor(out=rb[:, :], in0=rsum[:, :], in1=ra[:, :], op=ALU.mult))
    p.em('vector', lambda e: e.tensor_scalar(out=rb[:, :], in0=rb[:, :], scalar1=-1.0, scalar2=None, op0=ALU.mult))


def emit_bcast(p, ps, row_slice):
    p.em('tensor', lambda e, ps=ps, on=p.ones_row, r=row_slice: e.matmul(ps, on[0:1, :], r, start=True, stop=True))


def build_l0a():
    nc = bass.Bass("TRN2", target_bir_lowering=False)
    xT = nc.dram_tensor("xT", [RPC, D, T], F32, kind="ExternalInput").ap()
    qkvwT = nc.dram_tensor("qkvwT", [D, 3 * D], F32, kind="ExternalInput").ap()
    qkvb_row = nc.dram_tensor("qkvb_row", [1, 3 * D], F32, kind="ExternalInput").ap()
    vb_cols = nc.dram_tensor("vb_cols", [128, NC], F32, kind="ExternalInput").ap()
    projwT = nc.dram_tensor("projwT", [D, D], F32, kind="ExternalInput").ap()
    pb_cols = nc.dram_tensor("pb_cols", [128, NC], F32, kind="ExternalInput").ap()
    nw_cols = nc.dram_tensor("nw_cols", [128, NC], F32, kind="ExternalInput").ap()
    nb_cols = nc.dram_tensor("nb_cols", [128, NC], F32, kind="ExternalInput").ap()
    iden = nc.dram_tensor("iden", [128, 128], F32, kind="ExternalInput").ap()
    xnT = nc.dram_tensor("xnT", [RPC, D, T], F32, kind="ExternalOutput").ap()
    sT = nc.dram_tensor("sT", [RPC, D, T], F32, kind="ExternalOutput").ap()

    p = PB()
    scale = float(1.0 / np.sqrt(T))
    with ExitStack() as es:
        sb = lambda n, sh, dt=F32: es.enter_context(sbuf(nc,n, sh, dt))
        onc = sb("onc", [128, 1])
        onr = sb("onr", [1, 128])
        nwc = sb("nwc", [128, NC])
        nbc = sb("nbc", [128, NC])
        vbc = sb("vbc", [128, NC])
        pbc = sb("pbc", [128, NC])
        qbr = sb("qbr", [1, 3 * D])
        idb = sb("idb", [128, 128], BF16)
        sq = sb("sq", [128, T])
        rows = [sb("r%d" % i, [1, T]) for i in range(4)]
        mx = sb("mx", [128, 2])
        rs8 = sb("rs8", [128, NC])
        psA = es.enter_context(nc.psum_tensor("psA", [128, 512], F32))
        psB = es.enter_context(nc.psum_tensor("psB", [128, 512], F32))
        psTb = es.enter_context(nc.psum_tensor("psTb", [128, 128], BF16))
        psS = [es.enter_context(nc.psum_tensor("psS%d" % i, [1, 512], F32)) for i in range(4)]
        p.ones_col = onc
        p.ones_row = onr
        epsc = sb("epsc", [1, 1])
        p.eps_ap = epsc[0:1, 0:1]
        p.em('vector', lambda e: e.memset(onc[:, :], 1.0))
        p.em('vector', lambda e: e.memset(onr[:, :], 1.0))
        p.em('vector', lambda e: e.memset(epsc[:, :], 1e-5))
        p.dma(nwc[:, :], nw_cols[:, :])
        p.dma(nbc[:, :], nb_cols[:, :])
        p.dma(vbc[:, :], vb_cols[:, :])
        p.dma(pbc[:, :], pb_cols[:, :])
        p.dma(qbr[:, :], qkvb_row[:, :])
        ra, rb = rows[2], rows[3]

        for r in range(RPC):
            with sbuf(nc,"qn", [128, NTT * D], BF16) as qn, \
                 sbuf(nc,"kn", [128, NTT * D], BF16) as kn, \
                 sbuf(nc,"vT", [128, NC * T], BF16) as vT:
                with sbuf(nc,"hb", [128, NC * T], BF16) as hb:
                    with sbuf(nc,"xs", [128, T], F32) as xs, \
                         sbuf(nc,"xns", [128, T], F32) as xns, \
                         sbuf(nc,"tA", [128, 512], F32) as tA:
                        if r == 0:
                            p.dma(xs[:, 0:128], iden[:, :])
                            p.cast(idb[:, :], xs[:, 0:128])

                        def srcx(c):
                            p.dma(xs[:, :], xT[r, c * 128:(c + 1) * 128, :])
                            return xs[:, :]
                        emit_stats(p, srcx, sq, psS, rows)
                        for c in range(NC):
                            p.dma(xs[:, :], xT[r, c * 128:(c + 1) * 128, :])
                            for tw in range(NT4):
                                emit_bcast(p, psA[:, :], ra[:, tw * 512:(tw + 1) * 512])
                                emit_bcast(p, psB[:, :], rb[:, tw * 512:(tw + 1) * 512])
                                sl = xs[:, tw * 512:(tw + 1) * 512]
                                p.em('vector', lambda e, sl=sl: e.tensor_tensor(out=tA[:, :], in0=sl, in1=psA[:, :], op=ALU.mult))
                                p.em('vector', lambda e, o=xns[:, tw * 512:(tw + 1) * 512]: e.tensor_tensor(
                                    out=o, in0=tA[:, :], in1=psB[:, :], op=ALU.add))
                            p.dma(xnT[r, c * 128:(c + 1) * 128, :], xns[:, :])

                        def srcxn(c):
                            p.dma(xs[:, :], xnT[r, c * 128:(c + 1) * 128, :])
                            return xs[:, :]
                        emit_stats(p, srcxn, sq, psS, rows)
                        for c in range(NC):
                            p.dma(xs[:, :], xnT[r, c * 128:(c + 1) * 128, :])
                            for tw in range(NT4):
                                emit_bcast(p, psA[:, :], ra[:, tw * 512:(tw + 1) * 512])
                                emit_bcast(p, psB[:, :], rb[:, tw * 512:(tw + 1) * 512])
                                sl = xs[:, tw * 512:(tw + 1) * 512]
                                p.em('vector', lambda e, sl=sl: e.tensor_tensor(out=tA[:, :], in0=sl, in1=psA[:, :], op=ALU.mult))
                                p.em('vector', lambda e, o=xns[:, tw * 512:(tw + 1) * 512]: e.tensor_tensor(
                                    out=o, in0=tA[:, :], in1=psB[:, :], op=ALU.add))
                            p.em('vector', lambda e, c=c: e.tensor_scalar(
                                out=hb[:, c * T:(c + 1) * T], in0=xns[:, :], scalar1=nwc[:, c:c + 1],
                                scalar2=nbc[:, c:c + 1], op0=ALU.mult, op1=ALU.add))
                    with sbuf(nc,"wsc", [128, 512], F32) as wsc, \
                         sbuf(nc,"wcol", [128, NC * 512], BF16) as wcol:
                        for fh in range(4):
                            for c in range(NC):
                                p.dma(wsc[:, :], qkvwT[c * 128:(c + 1) * 128, fh * 512:(fh + 1) * 512])
                                p.cast(wcol[:, c * 512:(c + 1) * 512], wsc[:, :])
                            for tt in range(NTT):
                                fns = []
                                for c in range(NC):
                                    fns.append(lambda e, l=hb[:, c * T + tt * 128: c * T + tt * 128 + 128],
                                               rr=wcol[:, c * 512:(c + 1) * 512], s=(c == 0):
                                               e.matmul(psA[:, :], l, rr, start=s, stop=False))
                                fns.append(lambda e, rr=qbr[0:1, fh * 512:(fh + 1) * 512]: e.matmul(
                                    psA[:, :], onr[0:1, :], rr, start=False, stop=True))
                                p.grp('tensor', fns)
                                dst = qn if fh < 2 else kn
                                off = tt * D + (fh % 2) * 512
                                p.em('scalar', lambda e, o=dst[:, off:off + 512]: e.copy(out=o, in_=psA[:, :]))
                        for fc in range(NC):
                            for c in range(NC):
                                p.dma(wsc[:, 0:128], qkvwT[c * 128:(c + 1) * 128, 2 * D + fc * 128: 2 * D + (fc + 1) * 128])
                                p.cast(wcol[:, c * 512:c * 512 + 128], wsc[:, 0:128])
                            for tw in range(NT4):
                                pairs = [(wcol[:, c * 512:c * 512 + 128],
                                          hb[:, c * T + tw * 512: c * T + (tw + 1) * 512]) for c in range(NC)]
                                p.mm_group(psA[:, :], pairs)
                                p.em('scalar', lambda e, o=vT[:, fc * T + tw * 512: fc * T + (tw + 1) * 512], fc=fc:
                                     e.activation(out=o, in_=psA[:, :], func=AF.Identity, bias=vbc[:, fc:fc + 1]))
                with sbuf(nc,"aT", [128, NC * T], BF16) as aT, \
                     sbuf(nc,"Sf", [128, D], F32) as Sf, \
                     sbuf(nc,"Pb", [128, D], BF16) as Pb, \
                     sbuf(nc,"PTs", [128, D], BF16) as PTs:
                    for f1 in range(NC):
                        for half in range(2):
                            ps = psA if half == 0 else psB
                            pairs = [(qn[:, tt * D + f1 * 128: tt * D + (f1 + 1) * 128],
                                      kn[:, tt * D + half * 512: tt * D + (half + 1) * 512]) for tt in range(NTT)]
                            p.mm_group(ps[:, :], pairs)
                        p.em('scalar', lambda e: e.copy(out=Sf[:, 0:512], in_=psA[:, :]))
                        p.em('scalar', lambda e: e.copy(out=Sf[:, 512:1024], in_=psB[:, :]))
                        p.em('vector', lambda e: e.reduce_max(out=mx[:, 0:1], in_=Sf[:, :], axis=AX.X))
                        p.em('vector', lambda e, s=-scale: e.tensor_scalar(out=mx[:, 1:2], in0=mx[:, 0:1],
                                                                           scalar1=s, scalar2=None, op0=ALU.mult))
                        p.em('scalar', lambda e, s=scale, f1=f1: e.activation(
                            out=Pb[:, :], in_=Sf[:, :], func=AF.Exp, bias=mx[:, 1:2], scale=s,
                            accum_out=rs8[:, f1:f1 + 1]))
                        p.em('vector', lambda e, f1=f1: e.reciprocal(out=rs8[:, f1:f1 + 1], in_=rs8[:, f1:f1 + 1]))
                        for f2 in range(NC):
                            p.em('tensor', lambda e, i=Pb[:, f2 * 128:(f2 + 1) * 128]: e.transpose(psTb[:, :], i, idb[:, :]))
                            p.em('scalar', lambda e, o=PTs[:, f2 * 128:(f2 + 1) * 128]: e.copy(out=o, in_=psTb[:, :]))
                        for tw in range(NT4):
                            pairs = [(PTs[:, f2 * 128:(f2 + 1) * 128],
                                      vT[:, f2 * T + tw * 512: f2 * T + (tw + 1) * 512]) for f2 in range(NC)]
                            p.mm_group(psA[:, :], pairs)
                            p.em('scalar', lambda e, o=aT[:, f1 * T + tw * 512: f1 * T + (tw + 1) * 512], f1=f1:
                                 e.activation(out=o, in_=psA[:, :], func=AF.Identity, scale=rs8[:, f1:f1 + 1]))
                    with sbuf(nc,"wsc2", [128, 128], F32) as wsc2, \
                         sbuf(nc,"wpd", [128, NC * 128], BF16) as wpd, \
                         sbuf(nc,"sc", [128, T], F32) as sc, \
                         sbuf(nc,"xre", [128, T], F32) as xre:
                        for dc in range(NC):
                            for f1 in range(NC):
                                p.dma(wsc2[:, :], projwT[f1 * 128:(f1 + 1) * 128, dc * 128:(dc + 1) * 128])
                                p.cast(wpd[:, f1 * 128:(f1 + 1) * 128], wsc2[:, :])
                            p.dma(xre[:, :], xnT[r, dc * 128:(dc + 1) * 128, :])
                            for tw in range(NT4):
                                pairs = [(wpd[:, f1 * 128:(f1 + 1) * 128],
                                          aT[:, f1 * T + tw * 512: f1 * T + (tw + 1) * 512]) for f1 in range(NC)]
                                p.mm_group(psA[:, :], pairs)
                                p.em('vector', lambda e, dc=dc, x=xre[:, tw * 512:(tw + 1) * 512],
                                     o=sc[:, tw * 512:(tw + 1) * 512]: e.scalar_tensor_tensor(
                                    out=o, in0=psA[:, :], scalar=pbc[:, dc:dc + 1], in1=x, op0=ALU.add, op1=ALU.add))
                            p.dma(sT[r, dc * 128:(dc + 1) * 128, :], sc[:, :])
        p.finalize(nc, es)
    return nc


def build_l0b():
    nc = bass.Bass("TRN2", target_bir_lowering=False)
    xT = nc.dram_tensor("xT", [RPC, D, T], F32, kind="ExternalInput").ap()
    xnT = nc.dram_tensor("xnT", [RPC, D, T], F32, kind="ExternalInput").ap()
    sT = nc.dram_tensor("sT", [RPC, D, T], F32, kind="ExternalInput").ap()
    rw = nc.dram_tensor("rw", [D, 8], F32, kind="ExternalInput").ap()
    rb_row = nc.dram_tensor("rb_row", [1, 8], F32, kind="ExternalInput").ap()
    ew = nc.dram_tensor("ew", [8, D, D], F32, kind="ExternalInput").ap()
    eb_cols = nc.dram_tensor("eb_cols", [128, 64], F32, kind="ExternalInput").ap()
    iden = nc.dram_tensor("iden", [128, 128], F32, kind="ExternalInput").ap()
    x1T = nc.dram_tensor("x1T", [RPC, D, T], F32, kind="ExternalOutput").ap()
    xn1T = nc.dram_tensor("xn1T", [RPC, D, T], F32, kind="ExternalOutput").ap()
    mrD = nc.dram_tensor("mrD", [RPC * 8 * T], F32, kind="Internal").ap()
    mgD = nc.dram_tensor("mgD", [128, 128], F32, kind="Internal").ap()
    xnbD = nc.dram_tensor("xnbD", [RPC, D, T], BF16, kind="Internal").ap()

    p = PB()
    with ExitStack() as es:
        sb = lambda n, sh, dt=F32: es.enter_context(sbuf(nc,n, sh, dt))
        onc = sb("onc", [128, 1])
        onr = sb("onr", [1, 128])
        idf = sb("idf", [128, 128])
        rws = sb("rws", [128, NC * 8])
        rbs = sb("rbs", [1, 8])
        ebc = sb("ebc", [128, 64])
        rows = [sb("rr%d" % i, [1, T]) for i in range(4)]
        psA = es.enter_context(nc.psum_tensor("psA", [128, 512], F32))
        psB = es.enter_context(nc.psum_tensor("psB", [128, 512], F32))
        psT = es.enter_context(nc.psum_tensor("psT", [128, 128], F32))
        psS = [es.enter_context(nc.psum_tensor("psS%d" % i, [1, 512], F32)) for i in range(4)]
        p.ones_col = onc
        p.ones_row = onr
        epsc = sb("epsc", [1, 1])
        p.eps_ap = epsc[0:1, 0:1]
        p.em('vector', lambda e: e.memset(onc[:, :], 1.0))
        p.em('vector', lambda e: e.memset(onr[:, :], 1.0))
        p.em('vector', lambda e: e.memset(epsc[:, :], 1e-5))
        p.dma(idf[:, :], iden[:, :])
        p.dma(rbs[:, :], rb_row[:, :])
        p.dma(ebc[:, :], eb_cols[:, :])
        for c in range(NC):
            p.dma(rws[:, c * 8:(c + 1) * 8], rw[c * 128:(c + 1) * 128, :])

        with sbuf(nc,"xnr", [128, NC * T], F32) as xnr, \
             sbuf(nc,"lg", [128, 8], F32) as lg, \
             sbuf(nc,"mxr", [128, 2], F32) as mxr, \
             sbuf(nc,"gte", [128, 1], F32) as gte, \
             sbuf(nc,"msk", [128, 8], F32) as msk, \
             sbuf(nc,"mga", [128, 128], F32) as mga, \
             sbuf(nc,"xnbs", [128, T], BF16) as xnbs:
            for r in range(RPC):
                for c in range(NC):
                    p.dma(xnr[:, c * T:(c + 1) * T], xnT[r, c * 128:(c + 1) * 128, :])
                    p.cast(xnbs[:, :], xnr[:, c * T:(c + 1) * T])
                    p.dma(xnbD[r, c * 128:(c + 1) * 128, :], xnbs[:, :])
                for tt in range(NTT):
                    fns = []
                    for c in range(NC):
                        fns.append(lambda e, c=c, tt=tt: e.matmul(
                            psA[:, 0:8], xnr[:, c * T + tt * 128: c * T + tt * 128 + 128],
                            rws[:, c * 8:(c + 1) * 8], start=(c == 0), stop=False))
                    fns.append(lambda e: e.matmul(psA[:, 0:8], onr[0:1, :], rbs[0:1, :], start=False, stop=True))
                    p.grp('tensor', fns)
                    p.em('scalar', lambda e: e.copy(out=lg[:, :], in_=psA[:, 0:8]))
                    p.em('vector', lambda e: e.reduce_max(out=mxr[:, 0:1], in_=lg[:, :], axis=AX.X))
                    p.em('vector', lambda e: e.tensor_scalar(out=msk[:, :], in0=lg[:, :], scalar1=mxr[:, 0:1],
                                                             scalar2=None, op0=ALU.is_equal))
                    p.em('vector', lambda e: e.tensor_scalar(out=mxr[:, 1:2], in0=mxr[:, 0:1], scalar1=-1.0,
                                                             scalar2=None, op0=ALU.mult))
                    p.em('scalar', lambda e: e.activation(out=lg[:, :], in_=lg[:, :], func=AF.Exp,
                                                          bias=mxr[:, 1:2], accum_out=gte[:, :]))
                    p.em('vector', lambda e: e.reciprocal(out=gte[:, :], in_=gte[:, :]))
                    p.em('vector', lambda e, tt=tt: e.tensor_scalar(out=mga[:, tt * 8:(tt + 1) * 8], in0=msk[:, :],
                                                                    scalar1=gte[:, 0:1], scalar2=None, op0=ALU.mult))
                p.em('tensor', lambda e: e.transpose(psT[:, :], mga[:, :], idf[:, :]))
                p.em('scalar', lambda e: e.copy(out=mga[:, :], in_=psT[:, :]))
                p.dma(mgD[:, :], mga[:, :])
                for ei in range(8):
                    src = bass.AP(mgD.tensor, mgD.offset + ei * 128, [[1024, 16], [1, 128]])
                    dst = bass.AP(mrD.tensor, mrD.offset + (r * 8 + ei) * T, [[128, 16], [1, 128]])
                    p.dma(dst, src)

        with sbuf(nc,"ewb", [128, 64 * 128], BF16) as ewb, \
             sbuf(nc,"ews", [128, 128], F32) as ews, \
             sbuf(nc,"xnb", [128, NC * T], BF16) as xnb, \
             sbuf(nc,"mr", [1, 8 * T], F32) as mr, \
             sbuf(nc,"sel", [128, T], F32) as sel, \
             sbuf(nc,"ye", [128, 512], F32) as ye, \
             sbuf(nc,"tmp", [128, 512], F32) as tmp, \
             sbuf(nc,"xre", [128, T], F32) as xre, \
             sbuf(nc,"sre", [128, T], F32) as sre:
            for dc in range(NC):
                for ei in range(8):
                    for hc in range(NC):
                        p.dma(ews[:, :], ew[ei, hc * 128:(hc + 1) * 128, dc * 128:(dc + 1) * 128])
                        p.cast(ewb[:, (ei * 8 + hc) * 128:(ei * 8 + hc + 1) * 128], ews[:, :])
                for r in range(RPC):
                    for c in range(NC):
                        p.dma(xnb[:, c * T:(c + 1) * T], xnbD[r, c * 128:(c + 1) * 128, :])
                    p.dma(mr[:, :], bass.AP(mrD.tensor, mrD.offset + r * 8 * T, [[8 * T, 1], [1, 8 * T]]))
                    for tw in range(NT4):
                        for ei in range(8):
                            pairs = [(ewb[:, (ei * 8 + hc) * 128:(ei * 8 + hc + 1) * 128],
                                      xnb[:, hc * T + tw * 512: hc * T + (tw + 1) * 512]) for hc in range(NC)]
                            p.mm_group(psA[:, :], pairs)
                            p.em('scalar', lambda e, ei=ei, dc=dc: e.activation(
                                out=ye[:, :], in_=psA[:, :], func=AF.Identity, bias=ebc[:, ei * 8 + dc: ei * 8 + dc + 1]))
                            emit_bcast(p, psB[:, :], mr[0:1, ei * T + tw * 512: ei * T + (tw + 1) * 512])
                            p.em('vector', lambda e: e.tensor_tensor(out=tmp[:, :], in0=ye[:, :], in1=psB[:, :],
                                                                     op=ALU.mult))
                            if ei == 0:
                                p.em('vector', lambda e, tw=tw: e.tensor_copy(
                                    out=sel[:, tw * 512:(tw + 1) * 512], in_=tmp[:, :]))
                            else:
                                p.em('vector', lambda e, tw=tw: e.tensor_tensor(
                                    out=sel[:, tw * 512:(tw + 1) * 512], in0=sel[:, tw * 512:(tw + 1) * 512],
                                    in1=tmp[:, :], op=ALU.add))
                    p.dma(xre[:, :], xT[r, dc * 128:(dc + 1) * 128, :])
                    p.dma(sre[:, :], sT[r, dc * 128:(dc + 1) * 128, :])
                    p.em('scalar', lambda e: e.activation(out=sel[:, :], in_=sel[:, :], func=AF.Silu))
                    p.em('vector', lambda e: e.tensor_tensor(out=sel[:, :], in0=sel[:, :], in1=sre[:, :], op=ALU.mult))
                    p.em('vector', lambda e: e.tensor_tensor(out=sel[:, :], in0=sel[:, :], in1=xre[:, :], op=ALU.add))
                    p.dma(x1T[r, dc * 128:(dc + 1) * 128, :], sel[:, :])

        with sbuf(nc,"xs", [128, T], F32) as xs, \
             sbuf(nc,"xns", [128, T], F32) as xns, \
             sbuf(nc,"sq", [128, T], F32) as sq, \
             sbuf(nc,"tA", [128, 512], F32) as tA:
            ra, rb = rows[2], rows[3]
            for r in range(RPC):
                def srcx(c, r=r):
                    p.dma(xs[:, :], x1T[r, c * 128:(c + 1) * 128, :])
                    return xs[:, :]
                emit_stats(p, srcx, sq, psS, rows)
                for c in range(NC):
                    p.dma(xs[:, :], x1T[r, c * 128:(c + 1) * 128, :])
                    for tw in range(NT4):
                        emit_bcast(p, psA[:, :], ra[:, tw * 512:(tw + 1) * 512])
                        emit_bcast(p, psB[:, :], rb[:, tw * 512:(tw + 1) * 512])
                        sl = xs[:, tw * 512:(tw + 1) * 512]
                        p.em('vector', lambda e, sl=sl: e.tensor_tensor(out=tA[:, :], in0=sl, in1=psA[:, :], op=ALU.mult))
                        p.em('vector', lambda e, o=xns[:, tw * 512:(tw + 1) * 512]: e.tensor_tensor(
                            out=o, in0=tA[:, :], in1=psB[:, :], op=ALU.add))
                    p.dma(xn1T[r, c * 128:(c + 1) * 128, :], xns[:, :])
        p.finalize(nc, es)
    return nc


def build_l1a():
    nc = bass.Bass("TRN2", target_bir_lowering=False)
    xn1T = nc.dram_tensor("xn1T", [RPC, D, T], F32, kind="ExternalInput").ap()
    wdisp = nc.dram_tensor("wdisp", [D, 256], F32, kind="ExternalInput").ap()
    bdisp_cols = nc.dram_tensor("bdisp_cols", [128, 2], F32, kind="ExternalInput").ap()
    wval = nc.dram_tensor("wval", [D, D], F32, kind="ExternalInput").ap()
    bval_cols = nc.dram_tensor("bval_cols", [128, NC], F32, kind="ExternalInput").ap()
    bdm = nc.dram_tensor("bdm", [5, 128, 128], F32, kind="ExternalInput").ap()
    wbo_bd = nc.dram_tensor("wbo_bd", [128, 8], F32, kind="ExternalInput").ap()
    wdo_bd = nc.dram_tensor("wdo_bd", [128, 8], F32, kind="ExternalInput").ap()
    wpos_bd = nc.dram_tensor("wpos_bd", [128, 128], F32, kind="ExternalInput").ap()
    relT8 = nc.dram_tensor("relT8", [128, 32], F32, kind="ExternalInput").ap()
    bscols = nc.dram_tensor("bscols", [128, 3], F32, kind="ExternalInput").ap()
    bout8 = nc.dram_tensor("bout8", [8, 2], F32, kind="ExternalInput").ap()
    cm0 = nc.dram_tensor("cm0", [128, 4096], F32, kind="ExternalInput").ap()
    rc0 = nc.dram_tensor("rc0", [128, 128], F32, kind="ExternalInput").ap()
    am0 = nc.dram_tensor("am0", [8, 4096], F32, kind="ExternalInput").ap()
    selp = nc.dram_tensor("selp", [4, 8, 128], F32, kind="ExternalInput").ap()
    wcproj = nc.dram_tensor("wcproj", [D, D], F32, kind="ExternalInput").ap()
    bcp_cols = nc.dram_tensor("bcp_cols", [128, NC], F32, kind="ExternalInput").ap()
    s1T = nc.dram_tensor("s1T", [RPC, D, T], F32, kind="ExternalOutput").ap()

    TC = 128
    NTC = T // TC
    PW = 31 + T
    p = PB()
    with ExitStack() as es:
        sb = lambda n, sh, dt=F32: es.enter_context(sbuf(nc,n, sh, dt))
        wsc = sb("wsc", [128, 256])
        bdb = sb("bdb", [128, 5 * 128], BF16)
        wbo = sb("wbo", [128, 8], BF16)
        wdo = sb("wdo", [128, 8], BF16)
        pos8 = sb("pos8", [128, 32])
        bsc = sb("bsc", [128, 3])
        bo8 = sb("bo8", [8, 2])
        cm0s = sb("cm0s", [128, 4096])
        rc0s = sb("rc0s", [128, 128])
        am0s = sb("am0s", [8, 4096])
        selps = sb("selps", [8, 4 * 128])
        bdc = sb("bdc", [128, 2])
        bvc = sb("bvc", [128, NC])
        bcc = sb("bcc", [128, NC])
        wdb = sb("wdb", [128, NC * 256], BF16)
        dmgs = sb("dmgs", [8, 512])
        w8 = sb("w8", [8, TC * 32])
        m8 = sb("m8", [8, TC])
        s8 = sb("s8", [8, TC])
        onc = sb("onc", [128, 1])
        onr = sb("onr", [1, 128])
        psA = es.enter_context(nc.psum_tensor("psA", [128, 512], F32))
        psB = es.enter_context(nc.psum_tensor("psB", [128, 512], F32))
        psT = es.enter_context(nc.psum_tensor("psT", [128, 128], F32))
        p.ones_col = onc
        p.ones_row = onr
        epsc = sb("epsc", [1, 1])
        p.eps_ap = epsc[0:1, 0:1]
        p.em('vector', lambda e: e.memset(onc[:, :], 1.0))
        p.em('vector', lambda e: e.memset(onr[:, :], 1.0))
        p.em('vector', lambda e: e.memset(epsc[:, :], 1e-5))
        for i in range(5):
            p.dma(wsc[:, 0:128], bdm[i])
            p.cast(bdb[:, i * 128:(i + 1) * 128], wsc[:, 0:128])
        p.dma(wsc[:, 0:8], wbo_bd[:, :])
        p.cast(wbo[:, :], wsc[:, 0:8])
        p.dma(wsc[:, 0:8], wdo_bd[:, :])
        p.cast(wdo[:, :], wsc[:, 0:8])
        p.dma(bsc[:, :], bscols[:, :])
        p.dma(bo8[:, :], bout8[:, :])
        p.dma(cm0s[:, :], cm0[:, :])
        p.dma(rc0s[:, :], rc0[:, :])
        p.dma(am0s[:, :], am0[:, :])
        for pv in range(4):
            p.dma(selps[:, pv * 128:(pv + 1) * 128], selp[pv])
        p.dma(bdc[:, :], bdisp_cols[:, :])
        p.dma(bvc[:, :], bval_cols[:, :])
        p.dma(bcc[:, :], bcp_cols[:, :])
        for c in range(NC):
            p.dma(wsc[:, 0:256], wdisp[c * 128:(c + 1) * 128, :])
            p.cast(wdb[:, c * 256:(c + 1) * 256], wsc[:, 0:256])
        with sbuf(nc,"wpb", [128, 128], F32) as wpb, sbuf(nc,"rl8", [128, 32], F32) as rl8:
            p.dma(wpb[:, :], wpos_bd[:, :])
            p.dma(rl8[:, :], relT8[:, :])
            p.em('tensor', lambda e: e.matmul(psT[:, 0:32], wpb[:, :], rl8[:, :], start=True, stop=True))
            p.em('scalar', lambda e: e.copy(out=pos8[:, :], in_=psT[:, 0:32]))

        for r in range(RPC):
            with sbuf(nc,"dpd", [128, 2 * PW], F32) as dpd, \
                 sbuf(nc,"vpd", [128, NC * PW], BF16) as vpd, \
                 sbuf(nc,"OT", [128, NC * T], BF16) as OT:
                with sbuf(nc,"xnb", [128, NC * T], BF16) as xnb, \
                     sbuf(nc,"xsc", [128, T], F32) as xsc, \
                     sbuf(nc,"wvb", [128, NC * D], BF16) as wvb, \
                     sbuf(nc,"wv2", [128, D], F32) as wv2:
                    for c in range(NC):
                        p.dma(xsc[:, :], xn1T[r, c * 128:(c + 1) * 128, :])
                        p.cast(xnb[:, c * T:(c + 1) * T], xsc[:, :])
                        p.dma(wv2[:, :], wval[c * 128:(c + 1) * 128, :])
                        p.cast(wvb[:, c * D:(c + 1) * D], wv2[:, :])
                    p.em('vector', lambda e: e.memset(dpd[:, :], 0.0))
                    p.em('vector', lambda e: e.memset(vpd[:, :], 0.0))
                    for hg in range(2):
                        for tw in range(NT4):
                            pairs = [(wdb[:, c * 256 + hg * 128: c * 256 + (hg + 1) * 128],
                                      xnb[:, c * T + tw * 512: c * T + (tw + 1) * 512]) for c in range(NC)]
                            p.mm_group(psA[:, :], pairs)
                            p.em('scalar', lambda e, hg=hg, tw=tw: e.activation(
                                out=dpd[:, hg * PW + 31 + tw * 512: hg * PW + 31 + (tw + 1) * 512],
                                in_=psA[:, :], func=AF.Identity, bias=bdc[:, hg:hg + 1]))
                    for fc in range(NC):
                        for tw in range(NT4):
                            pairs = [(wvb[:, c * D + fc * 128: c * D + (fc + 1) * 128],
                                      xnb[:, c * T + tw * 512: c * T + (tw + 1) * 512]) for c in range(NC)]
                            p.mm_group(psA[:, :], pairs)
                            p.em('scalar', lambda e, fc=fc, tw=tw: e.activation(
                                out=vpd[:, fc * PW + 31 + tw * 512: fc * PW + 31 + (tw + 1) * 512],
                                in_=psA[:, :], func=AF.Identity, bias=bvc[:, fc:fc + 1]))
                with sbuf(nc,"spd", [128, PW], F32) as spd, \
                     sbuf(nc,"spdb", [128, PW], BF16) as spdb, \
                     sbuf(nc,"stb", [128, TC * 32], BF16) as stb, \
                     sbuf(nc,"swb", [128, TC * 32], BF16) as swb, \
                     sbuf(nc,"sfs", [128, TC * 32], F32) as sfs, \
                     sbuf(nc,"bab", [128, TC * 32], BF16) as bab, \
                     sbuf(nc,"dmb", [128, 512], BF16) as dmb, \
                     sbuf(nc,"sts", [128, TC], F32) as sts, \
                     sbuf(nc,"sis", [128, TC], F32) as sis, \
                     sbuf(nc,"tmf", [128, 512], F32) as tmf, \
                     sbuf(nc,"ots", [128, 16], F32) as ots:
                    for hg in range(2):
                        dp = dpd[:, hg * PW:(hg + 1) * PW]
                        p.em('vector', lambda e: e.memset(spd[:, :], 0.0))
                        p.em('vector', lambda e: e.memset(spdb[:, :], 0.0))
                        for tc in range(NTC):
                            t0 = tc * TC
                            p.em('vector', lambda e, dp=dp, t0=t0: e.tensor_tensor(
                                out=v3(stb[:, :]), in0=winap(dp, t0, TC),
                                in1=dp[:, 31 + t0: 31 + t0 + TC][:, :, None].broadcast_to([128, TC, 32]),
                                op=ALU.subtract))
                            for fr in range(8):
                                p.mm(psA[:, :], bdb[:, 0:128], stb[:, fr * 512:(fr + 1) * 512])
                                p.em('scalar', lambda e, fr=fr: e.activation(
                                    out=sfs[:, fr * 512:(fr + 1) * 512], in_=psA[:, :], func=AF.Identity,
                                    bias=bsc[:, 0:1]))
                            if tc == 0:
                                p.em('vector', lambda e: e.tensor_tensor(out=sfs[:, :], in0=sfs[:, :],
                                                                         in1=cm0s[:, :], op=ALU.mult))
                            p.em('vector', lambda e: e.tensor_reduce(out=sts[:, :, None], in_=v3(sfs[:, :]),
                                                                     axis=AX.X, op=ALU.add))
                            if tc == 0:
                                p.em('vector', lambda e, t0=t0: e.tensor_tensor(
                                    out=spd[:, 31 + t0: 31 + t0 + TC], in0=sts[:, :], in1=rc0s[:, :], op=ALU.mult))
                            else:
                                p.em('vector', lambda e, t0=t0: e.tensor_scalar(
                                    out=spd[:, 31 + t0: 31 + t0 + TC], in0=sts[:, :], scalar1=1.0 / 32.0,
                                    scalar2=None, op0=ALU.mult))
                        p.em('vector', lambda e: e.tensor_copy(out=spdb[:, 31:PW], in_=spd[:, 31:PW]))
                        for tc in range(NTC):
                            t0 = tc * TC
                            p.em('vector', lambda e, dp=dp, t0=t0: e.tensor_tensor(
                                out=v3(stb[:, :]), in0=winap(dp, t0, TC),
                                in1=dp[:, 31 + t0: 31 + t0 + TC][:, :, None].broadcast_to([128, TC, 32]),
                                op=ALU.subtract))
                            p.em('vector', lambda e, t0=t0: e.tensor_copy(out=v3(swb[:, :]),
                                                                          in_=winap(spdb[:, :], t0, TC)))
                            p.em('tensor', lambda e, t0=t0: e.matmul(psT[:, :], bdb[:, 2 * 128:3 * 128],
                                                                     spdb[:, 31 + t0: 31 + t0 + TC],
                                                                     start=True, stop=True))
                            p.em('scalar', lambda e: e.copy(out=sis[:, :], in_=psT[:, :]))
                            for fr in range(8):
                                fns = [lambda e, fr=fr: e.matmul(psA[:, :], bdb[:, 1 * 128:2 * 128],
                                                                 stb[:, fr * 512:(fr + 1) * 512],
                                                                 start=True, stop=False),
                                       lambda e, fr=fr: e.matmul(psA[:, :], bdb[:, 3 * 128:4 * 128],
                                                                 swb[:, fr * 512:(fr + 1) * 512],
                                                                 start=False, stop=True)]
                                p.grp('tensor', fns)
                                p.em('scalar', lambda e, fr=fr: e.copy(out=sfs[:, fr * 512:(fr + 1) * 512],
                                                                       in_=psA[:, :]))
                            p.em('vector', lambda e: e.tensor_tensor(
                                out=v3(sfs[:, :]), in0=v3(sfs[:, :]),
                                in1=sis[:, :, None].broadcast_to([128, TC, 32]), op=ALU.add))
                            p.em('vector', lambda e: e.tensor_tensor(
                                out=v3(sfs[:, :]), in0=v3(sfs[:, :]),
                                in1=pos8[:, None, :].broadcast_to([128, TC, 32]), op=ALU.add))
                            p.em('scalar', lambda e: e.activation(out=bab[:, :], in_=sfs[:, :], func=AF.Gelu,
                                                                  bias=bsc[:, 1:2]))
                            for fr in range(8):
                                p.mm(psB[0:8, :], wbo[:, :], bab[:, fr * 512:(fr + 1) * 512])
                                p.em('scalar', lambda e, fr=fr: e.activation(
                                    out=w8[:, fr * 512:(fr + 1) * 512], in_=psB[0:8, :], func=AF.Identity,
                                    bias=bo8[:, 0:1]))
                            for fr in range(8):
                                p.mm(psA[:, :], bdb[:, 4 * 128:5 * 128], stb[:, fr * 512:(fr + 1) * 512])
                                p.em('scalar', lambda e: e.activation(out=dmb[:, :], in_=psA[:, :], func=AF.Gelu,
                                                                      bias=bsc[:, 2:3]))
                                p.mm(psB[0:8, :], wdo[:, :], dmb[:, :])
                                p.em('scalar', lambda e: e.activation(out=dmgs[:, :], in_=psB[0:8, :],
                                                                      func=AF.Sigmoid, bias=bo8[:, 1:2]))
                                p.em('vector', lambda e, fr=fr: e.scalar_tensor_tensor(
                                    out=w8[:, fr * 512:(fr + 1) * 512], in0=dmgs[:, :], scalar=-10.0,
                                    in1=w8[:, fr * 512:(fr + 1) * 512], op0=ALU.mult, op1=ALU.add))
                            if tc == 0:
                                p.em('vector', lambda e: e.tensor_tensor(out=w8[:, :], in0=w8[:, :],
                                                                         in1=am0s[:, :], op=ALU.add))
                            p.em('vector', lambda e: e.reduce_max(out=m8[:, :, None], in_=v3(w8[:, :]), axis=AX.X))
                            p.em('vector', lambda e: e.tensor_tensor(out=v3(w8[:, :]), in0=v3(w8[:, :]),
                                                                     in1=m8[:, :, None].broadcast_to([8, TC, 32]),
                                                                     op=ALU.subtract))
                            p.em('scalar', lambda e: e.activation(out=w8[:, :], in_=w8[:, :], func=AF.Exp))
                            p.em('vector', lambda e: e.tensor_reduce(out=s8[:, :, None], in_=v3(w8[:, :]),
                                                                     axis=AX.X, op=ALU.add))
                            p.em('vector', lambda e: e.reciprocal(out=s8[:, :], in_=s8[:, :]))
                            p.em('vector', lambda e: e.tensor_tensor(out=v3(w8[:, :]), in0=v3(w8[:, :]),
                                                                     in1=s8[:, :, None].broadcast_to([8, TC, 32]),
                                                                     op=ALU.mult))
                            for vc in range(4):
                                vch = hg * 4 + vc
                                vp = vpd[:, vch * PW:(vch + 1) * PW]
                                for fr in range(8):
                                    p.mm(psB[:, :], selps[:, vc * 128:(vc + 1) * 128],
                                         w8[:, fr * 512:(fr + 1) * 512])
                                    p.em('vector', lambda e, vp=vp, c0=t0 + fr * 16: e.tensor_tensor(
                                        out=v3(tmf[:, :]), in0=winap(vp, c0, 16), in1=v3(psB[:, :]), op=ALU.mult))
                                    p.em('vector', lambda e: e.tensor_reduce(
                                        out=ots[:, :, None], in_=v3(tmf[:, :]), axis=AX.X, op=ALU.add))
                                    p.em('vector', lambda e, vch=vch, c0=t0 + fr * 16: e.tensor_copy(
                                        out=OT[:, vch * T + c0: vch * T + c0 + 16], in_=ots[:, :]))
                with sbuf(nc,"wcb", [128, NC * D], BF16) as wcb, \
                     sbuf(nc,"wc2", [128, D], F32) as wc2, \
                     sbuf(nc,"s1c", [128, T], F32) as s1c:
                    for c in range(NC):
                        p.dma(wc2[:, :], wcproj[c * 128:(c + 1) * 128, :])
                        p.cast(wcb[:, c * D:(c + 1) * D], wc2[:, :])
                    for dc in range(NC):
                        for tw in range(NT4):
                            pairs = [(wcb[:, fc * D + dc * 128: fc * D + (dc + 1) * 128],
                                      OT[:, fc * T + tw * 512: fc * T + (tw + 1) * 512]) for fc in range(NC)]
                            p.mm_group(psA[:, :], pairs)
                            p.em('scalar', lambda e, dc=dc, tw=tw: e.activation(
                                out=s1c[:, tw * 512:(tw + 1) * 512], in_=psA[:, :], func=AF.Identity,
                                bias=bcc[:, dc:dc + 1]))
                        p.dma(s1T[r, dc * 128:(dc + 1) * 128, :], s1c[:, :])
        p.finalize(nc, es)
    return nc


def build_l1f_attn():
    nc = bass.Bass("TRN2", target_bir_lowering=False)
    uT = nc.dram_tensor("uT", [RPC, D, T], F32, kind="ExternalInput").ap()
    wqkv = nc.dram_tensor("wqkv", [D, 3 * D], F32, kind="ExternalInput").ap()
    wo = nc.dram_tensor("wo", [D, D], F32, kind="ExternalInput").ap()
    qsc_d = nc.dram_tensor("qsc", [128, 1], F32, kind="ExternalInput").ap()
    ksc_d = nc.dram_tensor("ksc", [128, 1], F32, kind="ExternalInput").ap()
    cs2 = nc.dram_tensor("cs2", [128, T], F32, kind="ExternalInput").ap()
    ssf2 = nc.dram_tensor("ssf2", [128, T], F32, kind="ExternalInput").ap()
    ci2 = nc.dram_tensor("ci2", [128, T], F32, kind="ExternalInput").ap()
    sif2 = nc.dram_tensor("sif2", [128, T], F32, kind="ExternalInput").ap()
    msk = nc.dram_tensor("msk", [4, 128, 512], F32, kind="ExternalInput").ap()
    iden = nc.dram_tensor("iden", [128, 128], F32, kind="ExternalInput").ap()
    hmask = nc.dram_tensor("hmask", [128, 2], F32, kind="ExternalInput").ap()
    sel2d = nc.dram_tensor("sel2d", [2, 128], F32, kind="ExternalInput").ap()
    uoT = nc.dram_tensor("uoT", [RPC, D, T], F32, kind="ExternalOutput").ap()

    PK = 256 + T
    p = PB()
    with ExitStack() as es:
        sb = lambda n, sh, dt=F32: es.enter_context(sbuf(nc,n, sh, dt))
        cs2s = sb("cs2s", [128, T])
        ssf2s = sb("ssf2s", [128, T])
        ci2s = sb("ci2s", [128, T])
        sif2s = sb("sif2s", [128, T])
        msks = sb("msks", [128, 4 * 512])
        idb = sb("idb", [128, 128], BF16)
        hmk = sb("hmk", [128, 2])
        qscs = sb("qscs", [128, 1])
        kscs = sb("kscs", [128, 1])
        sel2 = sb("sel2", [2, 128])
        xsb = sb("xsb", [128, NC * T], BF16)
        OTb = sb("OTb", [128, NC * T], BF16)
        uscr = sb("uscr", [128, T])
        onc = sb("onc", [128, 1])
        onr = sb("onr", [1, 128])
        psA = es.enter_context(nc.psum_tensor("psA", [128, 512], F32))
        psB = es.enter_context(nc.psum_tensor("psB", [128, 512], F32))
        psTb = es.enter_context(nc.psum_tensor("psTb", [128, 128], BF16))
        psO = es.enter_context(nc.psum_tensor("psO", [64, 128], F32))
        psR = es.enter_context(nc.psum_tensor("psR", [2, 512], F32))
        p.ones_col = onc
        p.ones_row = onr
        epsc = sb("epsc", [1, 1])
        p.eps_ap = epsc[0:1, 0:1]
        p.em('vector', lambda e: e.memset(onc[:, :], 1.0))
        p.em('vector', lambda e: e.memset(onr[:, :], 1.0))
        p.em('vector', lambda e: e.memset(epsc[:, :], 1e-5))
        p.dma(cs2s[:, :], cs2[:, :])
        p.dma(ssf2s[:, :], ssf2[:, :])
        p.dma(ci2s[:, :], ci2[:, :])
        p.dma(sif2s[:, :], sif2[:, :])
        for i in range(4):
            p.dma(msks[:, i * 512:(i + 1) * 512], msk[i])
        p.dma(hmk[:, :], hmask[:, :])
        p.dma(qscs[:, :], qsc_d[:, :])
        p.dma(kscs[:, :], ksc_d[:, :])
        p.dma(uscr[:, 0:128], iden[:, :])
        p.cast(idb[:, :], uscr[:, 0:128])
        p.dma(sel2[:, :], sel2d[:, :])

        def emit_qk(qk, wqhp, qcf, sqc, ssr, qsn, t1, qbh, kph):
            tabc = cs2s if qk == 0 else ci2s
            tabs = ssf2s if qk == 0 else sif2s
            scl = qscs if qk == 0 else kscs
            for tw in range(NT4):
                pairs = [(wqhp[:, c * 384 + qk * 128: c * 384 + (qk + 1) * 128],
                          xsb[:, c * T + tw * 512: c * T + (tw + 1) * 512]) for c in range(NC)]
                p.mm_group(psA[:, :], pairs)
                p.em('scalar', lambda e, tw=tw: e.copy(out=qcf[:, tw * 512:(tw + 1) * 512], in_=psA[:, :]))
            p.em('scalar', lambda e: e.activation(out=sqc[:, :], in_=qcf[:, :], func=AF.Square))
            for tw in range(NT4):
                p.em('tensor', lambda e, tw=tw: e.matmul(psR[:, :], hmk[:, :],
                                                         sqc[:, tw * 512:(tw + 1) * 512], start=True, stop=True))
                p.em('scalar', lambda e, tw=tw: e.copy(out=ssr[:, tw * 512:(tw + 1) * 512], in_=psR[:, :]))
            p.em('scalar', lambda e: e.activation(out=ssr[:, :], in_=ssr[:, :], func=AF.Sqrt))
            p.em('vector', lambda e: e.tensor_scalar(out=ssr[:, :], in0=ssr[:, :], scalar1=1.0e-12,
                                                     scalar2=None, op0=ALU.max))
            p.em('vector', lambda e: e.reciprocal(out=ssr[:, :], in_=ssr[:, :]))
            for tw in range(NT4):
                p.em('tensor', lambda e, tw=tw: e.matmul(psB[:, :], sel2[:, :],
                                                         ssr[:, tw * 512:(tw + 1) * 512], start=True, stop=True))
                p.em('vector', lambda e, tw=tw: e.tensor_tensor(
                    out=qcf[:, tw * 512:(tw + 1) * 512], in0=qcf[:, tw * 512:(tw + 1) * 512],
                    in1=psB[:, :], op=ALU.mult))
            p.em('vector', lambda e, scl=scl: e.tensor_scalar(out=qcf[:, :], in0=qcf[:, :],
                                                              scalar1=scl[:, 0:1], scalar2=None, op0=ALU.mult))
            p.em('vector', lambda e: e.tensor_copy(out=qsn[0:32, :], in_=qcf[32:64, :]))
            p.em('vector', lambda e: e.tensor_copy(out=qsn[32:64, :], in_=qcf[0:32, :]))
            p.dma(qsn[64:96, :], qcf[96:128, :])
            p.dma(qsn[96:128, :], qcf[64:96, :])
            p.em('vector', lambda e, tabc=tabc: e.tensor_tensor(out=t1[:, :], in0=qcf[:, :],
                                                                in1=tabc[:, :], op=ALU.mult))
            p.em('vector', lambda e, tabs=tabs: e.tensor_tensor(out=qsn[:, :], in0=qsn[:, :],
                                                                in1=tabs[:, :], op=ALU.mult))
            if qk == 0:
                p.em('vector', lambda e: e.tensor_tensor(out=qbh[:, :], in0=t1[:, :], in1=qsn[:, :], op=ALU.add))
            else:
                p.em('vector', lambda e: e.tensor_tensor(out=kph[:, 256:PK], in0=t1[:, :], in1=qsn[:, :],
                                                         op=ALU.add))

        def emit_attn(hp, hf, w, it, qbh, kph, vbh, sf, mxc, rsc, pb2, ptb):
            i0 = w * 256 + it * 128
            p.em('tensor', lambda e, hf=hf, w=w, i0=i0: e.matmul(
                psA[:, :], qbh[64 * hf:64 * hf + 64, i0:i0 + 128],
                kph[64 * hf:64 * hf + 64, w * 256: w * 256 + 512], start=True, stop=True))
            mi = (2 if w == 0 else 0) + it
            p.em('vector', lambda e, mi=mi: e.tensor_tensor(
                out=sf[:, :], in0=psA[:, :], in1=msks[:, mi * 512:(mi + 1) * 512], op=ALU.add))
            p.em('vector', lambda e: e.reduce_max(out=mxc[:, 0:1], in_=sf[:, :], axis=AX.X))
            p.em('vector', lambda e: e.tensor_scalar(out=mxc[:, 1:2], in0=mxc[:, 0:1],
                                                     scalar1=-8.0, scalar2=None, op0=ALU.mult))
            p.em('scalar', lambda e: e.activation(out=pb2[:, :], in_=sf[:, :], func=AF.Exp,
                                                  bias=mxc[:, 1:2], scale=8.0, accum_out=rsc[:, :]))
            p.em('vector', lambda e: e.reciprocal(out=rsc[:, :], in_=rsc[:, :]))
            p.em('vector', lambda e: e.tensor_scalar(out=pb2[:, :], in0=pb2[:, :],
                                                     scalar1=rsc[:, 0:1], scalar2=None, op0=ALU.mult))
            for jc in range(4):
                p.em('tensor', lambda e, jc=jc: e.transpose(psTb[:, :], pb2[:, jc * 128:(jc + 1) * 128],
                                                            idb[:, :]))
                p.em('scalar', lambda e, jc=jc: e.copy(out=ptb[:, jc * 128:(jc + 1) * 128], in_=psTb[:, :]))
            jcs = [jc for jc in range(4) if not (w == 0 and jc < 2)]
            fns = []
            for idx, jc in enumerate(jcs):
                jch = (w - 1) * 2 + jc
                fns.append(lambda e, jch=jch, jc=jc, hf=hf, s=(idx == 0), t=(idx == len(jcs) - 1): e.matmul(
                    psO[:, :], vbh[:, jch * 128 + 64 * hf: jch * 128 + 64 * hf + 64],
                    ptb[:, jc * 128:(jc + 1) * 128], start=s, stop=t))
            p.grp('tensor', fns)
            p.em('scalar', lambda e, hf=hf, hp=hp, i0=i0: e.copy(
                out=OTb[64 * hf:64 * hf + 64, hp * T + i0: hp * T + i0 + 128], in_=psO[:, :]))

        def emit_hp(hp):
            with sbuf(nc,"wqs", [128, 384], F32) as wqs, \
                 sbuf(nc,"wqhp", [128, NC * 384], BF16) as wqhp, \
                 sbuf(nc,"qbh", [128, T], BF16) as qbh, \
                 sbuf(nc,"kph", [128, PK], BF16) as kph, \
                 sbuf(nc,"vbh", [128, NTT * 128], BF16) as vbh, \
                 sbuf(nc,"qcf", [128, T], F32) as qcf, \
                 sbuf(nc,"qsn", [128, T], F32) as qsn, \
                 sbuf(nc,"sqc", [128, T], F32) as sqc, \
                 sbuf(nc,"t1", [128, T], F32) as t1, \
                 sbuf(nc,"ssr", [2, T], F32) as ssr, \
                 sbuf(nc,"sf", [128, 512], F32) as sf, \
                 sbuf(nc,"mxc", [128, 2], F32) as mxc, \
                 sbuf(nc,"rsc", [128, 1], F32) as rsc, \
                 sbuf(nc,"pb2", [128, 512], BF16) as pb2, \
                 sbuf(nc,"ptb", [128, 512], BF16) as ptb:
                for c in range(NC):
                    for s3 in range(3):
                        p.dma(wqs[:, s3 * 128:(s3 + 1) * 128],
                              wqkv[c * 128:(c + 1) * 128, s3 * D + hp * 128: s3 * D + hp * 128 + 128])
                    p.cast(wqhp[:, c * 384:(c + 1) * 384], wqs[:, :])
                p.em('vector', lambda e: e.memset(kph[:, 0:256], 0.0))
                for qk in range(2):
                    emit_qk(qk, wqhp, qcf, sqc, ssr, qsn, t1, qbh, kph)
                for tt in range(NTT):
                    pairs = [(xsb[:, c * T + tt * 128: c * T + tt * 128 + 128],
                              wqhp[:, c * 384 + 256: c * 384 + 384]) for c in range(NC)]
                    p.mm_group(psA[:, 0:128], pairs)
                    p.em('scalar', lambda e, tt=tt: e.copy(out=vbh[:, tt * 128:(tt + 1) * 128],
                                                           in_=psA[:, 0:128]))
                for hf in range(2):
                    for w in range(8):
                        for it in range(2):
                            emit_attn(hp, hf, w, it, qbh, kph, vbh, sf, mxc, rsc, pb2, ptb)

        for r in range(RPC):
            for c in range(NC):
                p.dma(uscr[:, :], uT[r, c * 128:(c + 1) * 128, :])
                if c < 4:
                    p.cast(xsb[:, c * T:(c + 1) * T], uscr[:, :])
                else:
                    p.em('vector', lambda e, c=c: e.memset(xsb[:, c * T:c * T + 1], 0.0))
                    p.cast(xsb[:, c * T + 1:(c + 1) * T], uscr[:, 0:T - 1])
            for hp in range(8):
                emit_hp(hp)
            with sbuf(nc,"wos", [128, 128], F32) as wos, \
                 sbuf(nc,"wod", [128, NC * 128], BF16) as wod:
                for dc in range(NC):
                    for c in range(NC):
                        p.dma(wos[:, :], wo[c * 128:(c + 1) * 128, dc * 128:(dc + 1) * 128])
                        p.cast(wod[:, c * 128:(c + 1) * 128], wos[:, :])
                    p.dma(uscr[:, :], uT[r, dc * 128:(dc + 1) * 128, :])
                    for tw in range(NT4):
                        pairs = [(wod[:, fc * 128:(fc + 1) * 128],
                                  OTb[:, fc * T + tw * 512: fc * T + (tw + 1) * 512]) for fc in range(NC)]
                        p.mm_group(psA[:, :], pairs)
                        sl = uscr[:, tw * 512:(tw + 1) * 512]
                        p.em('vector', lambda e, sl=sl: e.tensor_tensor(out=sl, in0=psA[:, :], in1=sl, op=ALU.add))
                    p.dma(uoT[r, dc * 128:(dc + 1) * 128, :], uscr[:, :])
        p.finalize(nc, es)
    return nc


K_CH = [(k * 128, 128) for k in range(21)] + [(2688, 42)]


def build_l1f_ffn(final=False):
    nc = bass.Bass("TRN2", target_bir_lowering=False)
    uT = nc.dram_tensor("uT", [RPC, D, T], F32, kind="ExternalInput").ap()
    w1 = nc.dram_tensor("w1", [D, 5460], F32, kind="ExternalInput").ap()
    w2 = nc.dram_tensor("w2", [2730, D], F32, kind="ExternalInput").ap()
    b1c = nc.dram_tensor("b1c", [128, 44], F32, kind="ExternalInput").ap()
    b2c = nc.dram_tensor("b2c", [128, NC], F32, kind="ExternalInput").ap()
    lngc = nc.dram_tensor("lngc", [128, NC], F32, kind="ExternalInput").ap()
    lnbc = nc.dram_tensor("lnbc", [128, NC], F32, kind="ExternalInput").ap()
    if final:
        x1T = nc.dram_tensor("x1T", [RPC, D, T], F32, kind="ExternalInput").ap()
        s1T = nc.dram_tensor("s1T", [RPC, D, T], F32, kind="ExternalInput").ap()
        outT = nc.dram_tensor("outT", [RPC, D, T], F32, kind="ExternalOutput").ap()
    else:
        uoT = nc.dram_tensor("uoT", [RPC, D, T], F32, kind="ExternalOutput").ap()

    p = PB()
    with ExitStack() as es:
        sb = lambda n, sh, dt=F32: es.enter_context(sbuf(nc,n, sh, dt))
        u = sb("u", [128, NC * T])
        xnb = sb("xnb", [128, NC * T], BF16)
        rows = [sb("r%d" % i, [1, T]) for i in range(4)]
        b1s = sb("b1s", [128, 44])
        b2s = sb("b2s", [128, NC])
        lgs = sb("lgs", [128, NC])
        lbs = sb("lbs", [128, NC])
        onc = sb("onc", [128, 1])
        onr = sb("onr", [1, 128])
        psA = es.enter_context(nc.psum_tensor("psA", [128, 512], F32))
        psB = es.enter_context(nc.psum_tensor("psB", [128, 512], F32))
        psS = [es.enter_context(nc.psum_tensor("psS%d" % i, [1, 512], F32)) for i in range(4)]
        p.ones_col = onc
        p.ones_row = onr
        epsc = sb("epsc", [1, 1])
        p.eps_ap = epsc[0:1, 0:1]
        p.em('vector', lambda e: e.memset(onc[:, :], 1.0))
        p.em('vector', lambda e: e.memset(onr[:, :], 1.0))
        p.em('vector', lambda e: e.memset(epsc[:, :], 1e-5))
        p.dma(b1s[:, :], b1c[:, :])
        p.dma(b2s[:, :], b2c[:, :])
        p.dma(lgs[:, :], lngc[:, :])
        p.dma(lbs[:, :], lnbc[:, :])
        ra, rb = rows[2], rows[3]

        for r in range(RPC):
            for c in range(NC):
                p.dma(u[:, c * T:(c + 1) * T], uT[r, c * 128:(c + 1) * 128, :])
            with sbuf(nc,"xsh", [128, 4 * T], F32) as xsh, \
                 sbuf(nc,"sq", [128, T], F32) as sq, \
                 sbuf(nc,"tA", [128, 512], F32) as tA:
                for c in range(4, NC):
                    j = c - 4
                    p.em('vector', lambda e, j=j: e.memset(xsh[:, j * T:j * T + 1], 0.0))
                    p.em('vector', lambda e, c=c, j=j: e.tensor_copy(out=xsh[:, j * T + 1:(j + 1) * T],
                                                                     in_=u[:, c * T:(c + 1) * T - 1]))
                cap = lambda c: (u[:, c * T:(c + 1) * T] if c < 4 else xsh[:, (c - 4) * T:(c - 3) * T])
                emit_stats(p, lambda c: cap(c), sq, psS, rows)
                for c in range(NC):
                    for tw in range(NT4):
                        emit_bcast(p, psA[:, :], ra[:, tw * 512:(tw + 1) * 512])
                        emit_bcast(p, psB[:, :], rb[:, tw * 512:(tw + 1) * 512])
                        sl = cap(c)[:, tw * 512:(tw + 1) * 512]
                        p.em('vector', lambda e, sl=sl: e.tensor_tensor(out=tA[:, :], in0=sl, in1=psA[:, :],
                                                                        op=ALU.mult))
                        p.em('vector', lambda e: e.tensor_tensor(out=tA[:, :], in0=tA[:, :], in1=psB[:, :],
                                                                 op=ALU.add))
                        p.em('vector', lambda e, c=c, tw=tw: e.tensor_scalar(
                            out=xnb[:, c * T + tw * 512: c * T + (tw + 1) * 512], in0=tA[:, :],
                            scalar1=lgs[:, c:c + 1], scalar2=lbs[:, c:c + 1], op0=ALU.mult, op1=ALU.add))
            for dc in range(NC):
                p.em('vector', lambda e, dc=dc: e.tensor_scalar(out=u[:, dc * T:(dc + 1) * T],
                                                                in0=u[:, dc * T:(dc + 1) * T],
                                                                scalar1=b2s[:, dc:dc + 1], scalar2=None,
                                                                op0=ALU.add))
            with sbuf(nc,"w2s", [128, D], F32) as w2s, \
                 sbuf(nc,"w2b", [128, D], BF16) as w2b, \
                 sbuf(nc,"w1s", [128, 128], F32) as w1s, \
                 sbuf(nc,"w1cb", [128, NC * 128], BF16) as w1cb, \
                 sbuf(nc,"afc", [128, T], F32) as afc, \
                 sbuf(nc,"gfc", [128, T], F32) as gfc, \
                 sbuf(nc,"h2b", [128, T], BF16) as h2b:
                for k, (off, cnt) in enumerate(K_CH):
                    for (oo, colidx, dstf, fn) in ((off, 2 * k, afc, AF.Identity),
                                                   (2730 + off, 2 * k + 1, gfc, AF.Gelu)):
                        for c in range(NC):
                            p.dma(w1s[:, 0:cnt], w1[c * 128:(c + 1) * 128, oo:oo + cnt])
                            p.cast(w1cb[:, c * 128:c * 128 + cnt], w1s[:, 0:cnt])
                        for tw in range(NT4):
                            pairs = [(w1cb[:, c * 128:c * 128 + cnt],
                                      xnb[:, c * T + tw * 512: c * T + (tw + 1) * 512]) for c in range(NC)]
                            p.mm_group(psA[0:cnt, :], pairs)
                            p.em('scalar', lambda e, tw=tw, dstf=dstf, fn=fn, colidx=colidx, cnt=cnt:
                                 e.activation(out=dstf[0:cnt, tw * 512:(tw + 1) * 512], in_=psA[0:cnt, :],
                                              func=fn, bias=b1s[0:cnt, colidx:colidx + 1]))
                    p.em('vector', lambda e, cnt=cnt: e.tensor_tensor(out=h2b[0:cnt, :], in0=afc[0:cnt, :],
                                                                      in1=gfc[0:cnt, :], op=ALU.mult))
                    p.dma(w2s[0:cnt, :], w2[off:off + cnt, :])
                    p.cast(w2b[0:cnt, :], w2s[0:cnt, :])
                    for dc in range(NC):
                        for tw in range(NT4):
                            p.em('tensor', lambda e, dc=dc, tw=tw, cnt=cnt: e.matmul(
                                psA[:, :], w2b[0:cnt, dc * 128:(dc + 1) * 128],
                                h2b[0:cnt, tw * 512:(tw + 1) * 512], start=True, stop=True))
                            sl = u[:, dc * T + tw * 512: dc * T + (tw + 1) * 512]
                            p.em('vector', lambda e, sl=sl: e.tensor_tensor(out=sl, in0=psA[:, :], in1=sl,
                                                                            op=ALU.add))
            if final:
                with sbuf(nc,"x1s", [128, T], F32) as x1s, \
                     sbuf(nc,"s1s", [128, T], F32) as s1s:
                    for dc in range(NC):
                        p.dma(x1s[:, :], x1T[r, dc * 128:(dc + 1) * 128, :])
                        p.dma(s1s[:, :], s1T[r, dc * 128:(dc + 1) * 128, :])
                        sl = u[:, dc * T:(dc + 1) * T]
                        p.em('scalar', lambda e, sl=sl: e.activation(out=sl, in_=sl, func=AF.Silu))
                        p.em('vector', lambda e, sl=sl: e.tensor_tensor(out=sl, in0=sl, in1=s1s[:, :], op=ALU.mult))
                        p.em('vector', lambda e, sl=sl: e.tensor_tensor(out=sl, in0=sl, in1=x1s[:, :], op=ALU.add))
                        p.dma(outT[r, dc * 128:(dc + 1) * 128, :], sl)
            else:
                for dc in range(NC):
                    p.dma(uoT[r, dc * 128:(dc + 1) * 128, :], u[:, dc * T:(dc + 1) * T])
        p.finalize(nc, es)
    return nc


def _rotary_tables():
    DH, W, n = 64, 256, T
    half = np.arange(0, DH, 2, dtype=np.float64)
    inv_freq = 1.0 / 10000.0 ** (half / DH)
    t = np.arange(n, dtype=np.float64)
    freqs = t[:, None] * inv_freq[None, :]
    freqs = np.concatenate([freqs, freqs], -1)
    base = (half + 0.4 * DH) / (1.4 * DH)
    power = (t - n // 2) / (W // 2)
    scale = base[None, :] ** power[:, None]
    scale = np.concatenate([scale, scale], -1)
    cos, sin = np.cos(freqs), np.sin(freqs)
    sgn = np.ones(DH)
    sgn[:DH // 2] = -1.0
    cs = (cos * scale).T
    ssf = (sin * scale).T * sgn[:, None]
    ci = (cos / scale).T
    sif = (sin / scale).T * sgn[:, None]
    f = lambda a: np.tile(a, (2, 1)).astype(np.float32)
    return f(cs), f(ssf), f(ci), f(sif)


def _local_masks():
    out = []
    for w0 in (False, True):
        for it in (0, 1):
            m = np.full((128, 512), NEG, dtype=np.float32)
            for pp in range(128):
                i = it * 128 + pp
                lo, hi = i, i + 256
                if w0:
                    lo = max(lo, 256)
                m[pp, lo:min(hi + 1, 512)] = 0.0
            out.append(m)
    return np.stack(out)


def _cols(v):
    return np.ascontiguousarray(v.reshape(NC, 128).T)


def kernel(**inp):
    f32 = np.float32
    x = np.asarray(inp["L0_attn_x"], dtype=f32)
    xT = np.ascontiguousarray(x.transpose(0, 2, 1))
    xT_pc = [np.ascontiguousarray(xT[c * RPC:(c + 1) * RPC]) for c in range(NCORES)]

    iden = np.eye(128, dtype=f32)
    l0a_static = dict(
        qkvwT=np.ascontiguousarray(np.asarray(inp["L0_attn_qkv_w"]).T),
        qkvb_row=np.ascontiguousarray(np.asarray(inp["L0_attn_qkv_b"]).reshape(1, 3 * D)),
        vb_cols=_cols(np.asarray(inp["L0_attn_qkv_b"])[2 * D:]),
        projwT=np.ascontiguousarray(np.asarray(inp["L0_attn_proj_w"]).T),
        pb_cols=_cols(np.asarray(inp["L0_attn_proj_b"])),
        nw_cols=_cols(np.asarray(inp["L0_attn_norm_w"])),
        nb_cols=_cols(np.asarray(inp["L0_attn_norm_b"])),
        iden=iden,
    )
    eb = np.asarray(inp["L0_ffn_expert_b"])
    eb_cols = np.ascontiguousarray(eb.reshape(8, NC, 128).transpose(2, 0, 1).reshape(128, 64))
    l0b_static = dict(
        rw=np.ascontiguousarray(np.asarray(inp["L0_ffn_router_w"])),
        rb_row=np.ascontiguousarray(np.asarray(inp["L0_ffn_router_b"]).reshape(1, 8)),
        ew=np.ascontiguousarray(np.asarray(inp["L0_ffn_expert_w"])),
        eb_cols=eb_cols,
        iden=iden,
    )

    def bd8(w):
        m = np.zeros((128, 128), dtype=f32)
        for h in range(8):
            m[16 * h:16 * h + 16, 16 * h:16 * h + 16] = w
        return m

    def bdcol(w):
        m = np.zeros((128, 8), dtype=f32)
        for h in range(8):
            m[16 * h:16 * h + 16, h] = w[:, 0]
        return m

    TCa = 128
    tloc = np.arange(TCa)
    ddv = np.arange(32)
    cm = ((tloc[:, None] + ddv[None, :]) >= 31).astype(f32)
    cm0 = np.broadcast_to(cm.reshape(1, TCa * 32), (128, TCa * 32)).copy()
    rc = (1.0 / np.minimum(tloc + 1, 32)).astype(f32)
    rc0 = np.broadcast_to(rc.reshape(1, TCa), (128, TCa)).copy()
    am = np.where(cm > 0, 0.0, NEG).astype(f32)
    am0 = np.broadcast_to(am.reshape(1, TCa * 32), (8, TCa * 32)).copy()
    selp = np.zeros((4, 8, 128), dtype=f32)
    for pv in range(4):
        selp[pv, 2 * pv, 0:64] = 1.0
        selp[pv, 2 * pv + 1, 64:128] = 1.0
    l1a_static = dict(
        wdisp=np.ascontiguousarray(np.asarray(inp["L1_attn_w_disp"])),
        bdisp_cols=np.ascontiguousarray(np.asarray(inp["L1_attn_b_disp"]).reshape(2, 128).T),
        wval=np.ascontiguousarray(np.asarray(inp["L1_attn_w_val"])),
        bval_cols=_cols(np.asarray(inp["L1_attn_b_val"])),
        bdm=np.stack([bd8(np.asarray(inp["L1_attn_w_state"])), bd8(np.asarray(inp["L1_attn_w_strain"])),
                      bd8(np.asarray(inp["L1_attn_w_state_i"])), bd8(np.asarray(inp["L1_attn_w_state_j"])),
                      bd8(np.asarray(inp["L1_attn_w_damage"]))]),
        wbo_bd=bdcol(np.asarray(inp["L1_attn_w_bond_out"])),
        wdo_bd=bdcol(np.asarray(inp["L1_attn_w_damage_out"])),
        wpos_bd=bd8(np.asarray(inp["L1_attn_w_pos"])),
        relT8=np.ascontiguousarray(np.tile(np.asarray(inp["L1_attn_rel_pos_emb"])[:32].T, (8, 1))),
        bscols=np.stack([np.tile(np.asarray(inp["L1_attn_b_state"]), 8),
                         np.tile(np.asarray(inp["L1_attn_b_strain"]), 8),
                         np.tile(np.asarray(inp["L1_attn_b_damage"]), 8)], axis=1).astype(f32),
        bout8=np.stack([np.tile(np.asarray(inp["L1_attn_b_bond_out"]), 8),
                        np.tile(np.asarray(inp["L1_attn_b_damage_out"]), 8)], axis=1).astype(f32),
        cm0=cm0, rc0=rc0, am0=am0, selp=selp,
        wcproj=np.ascontiguousarray(np.asarray(inp["L1_attn_w_cproj"])),
        bcp_cols=_cols(np.asarray(inp["L1_attn_b_cproj"])),
    )
    cs2, ssf2, ci2, sif2 = _rotary_tables()
    msk = _local_masks()
    hmask = np.zeros((128, 2), dtype=f32)
    hmask[0:64, 0] = 1.0
    hmask[64:128, 1] = 1.0
    sel2h = np.zeros((2, 128), dtype=f32)
    sel2h[0, 0:64] = 1.0
    sel2h[1, 64:128] = 1.0
    l1f_attn_static = []
    l1f_ffn_static = []
    for i in range(6):
        l1f_attn_static.append(dict(
            wqkv=np.ascontiguousarray(np.asarray(inp["L1_ffn_Wqkv"])[i]),
            wo=np.ascontiguousarray(np.asarray(inp["L1_ffn_Wo"])[i]),
            qsc=np.tile(np.asarray(inp["L1_ffn_q_scale"])[i], 2).reshape(128, 1).astype(f32),
            ksc=np.tile(np.asarray(inp["L1_ffn_k_scale"])[i], 2).reshape(128, 1).astype(f32),
            cs2=cs2, ssf2=ssf2, ci2=ci2, sif2=sif2, msk=msk, iden=iden, hmask=hmask, sel2d=sel2h,
        ))
        b1 = np.asarray(inp["L1_ffn_b1"])[i]
        b1cols = np.zeros((128, 44), dtype=f32)
        for k, (off, cnt) in enumerate(K_CH):
            b1cols[0:cnt, 2 * k] = b1[off:off + cnt]
            b1cols[0:cnt, 2 * k + 1] = b1[2730 + off:2730 + off + cnt]
        l1f_ffn_static.append(dict(
            w1=np.ascontiguousarray(np.asarray(inp["L1_ffn_W1"])[i]),
            w2=np.ascontiguousarray(np.asarray(inp["L1_ffn_W2"])[i]),
            b1c=b1cols,
            b2c=_cols(np.asarray(inp["L1_ffn_b2"])[i]),
            lngc=_cols(np.asarray(inp["L1_ffn_ln_g"])[i]),
            lnbc=_cols(np.asarray(inp["L1_ffn_ln_b"])[i]),
        ))

    cores = list(range(NCORES))

    def launch(nc, static, per_core, outs):
        in_maps = []
        for c in cores:
            m = dict(static)
            for k, v in per_core.items():
                m[k] = v[c]
            in_maps.append(m)
        res = run_bass_kernel_spmd(nc, in_maps, core_ids=cores)
        return {k: [res.results[c][k] for c in cores] for k in outs}

    nc_l0a = build_l0a()
    r0 = launch(nc_l0a, l0a_static, {"xT": xT_pc}, ["xnT", "sT"])
    nc_l0b = build_l0b()
    r1 = launch(nc_l0b, l0b_static, {"xT": xT_pc, "xnT": r0["xnT"], "sT": r0["sT"]}, ["x1T", "xn1T"])
    nc_l1a = build_l1a()
    r2 = launch(nc_l1a, l1a_static, {"xn1T": r1["xn1T"]}, ["s1T"])
    nc_a = build_l1f_attn()
    nc_f = build_l1f_ffn(final=False)
    nc_ff = build_l1f_ffn(final=True)
    ut = r1["xn1T"]
    rf = None
    for i in range(6):
        ra = launch(nc_a, l1f_attn_static[i], {"uT": ut}, ["uoT"])
        if i < 5:
            rf = launch(nc_f, l1f_ffn_static[i], {"uT": ra["uoT"]}, ["uoT"])
            ut = rf["uoT"]
        else:
            rf = launch(nc_ff, l1f_ffn_static[i],
                        {"uT": ra["uoT"], "x1T": r1["x1T"], "s1T": r2["s1T"]}, ["outT"])
    outT = np.concatenate(rf["outT"], axis=0)
    return np.ascontiguousarray(outT.transpose(0, 2, 1)).astype(np.float32)
